# version 04 (final)
# baseline (speedup 1.0000x reference)
import numpy as np
from contextlib import ExitStack
import ml_dtypes
import concourse.bass as bass
import concourse.mybir as mybir
from concourse.bass_utils import run_bass_kernel_spmd

F32 = mybir.dt.float32
BF16 = mybir.dt.bfloat16
I32 = mybir.dt.int32
AF = mybir.ActivationFunctionType
ALU = mybir.AluOpType
AX = mybir.AxisListType
BF16NP = ml_dtypes.bfloat16

ENGS = ["sync", "tensor", "vector", "scalar", "gpsimd"]
NCORE = 8
B, L, D = 4, 2048, 4096
DIN = 8192
NH, HD = 128, 64
G, N = 8, 128
CONV_DIM = DIN + 2 * G * N
PROJ = DIN + CONV_DIM + NH
CS = 256
NCH = L // CS
INTER = 11008
EPS = 1e-5


def cdiv(a, b):
    return (a + b - 1) // b


class Prog:
    CHANS = ENGS + ["wA", "wB"]

    def __init__(self, nc):
        self.nc = nc
        self.ops = []
        self.counts = {e: 0 for e in self.CHANS}
        self.last = None
        self.sems = {}
        self._ctxs = []

    def emit(self, eng, fn, dma=False, deps="prev", chan=None):
        inc = 16 if dma else 1
        chan = chan or eng
        if deps == "prev":
            waits = [self.last] if self.last is not None else []
        else:
            waits = [d for d in deps if d is not None]
        self.counts[chan] += inc
        self.ops.append((eng, chan, waits, fn, inc))
        self.last = (chan, self.counts[chan])
        return self.last

    def build(self):
        nc = self.nc
        for e in self.CHANS:
            cm = nc.semaphore(f"sem_{e}")
            self.sems[e] = cm.__enter__()
            self._ctxs.append(cm)
        blk = nc.Block()
        block = blk.__enter__()
        for e in ENGS:
            def section(engine, _e=e):
                hi = {x: 0 for x in self.CHANS}
                for (eng, chan, waits, fn, inc) in self.ops:
                    if eng != _e:
                        continue
                    for (we, wc) in waits:
                        if we == "tensor" and _e == "tensor":
                            continue
                        if wc > hi[we]:
                            engine.wait_ge(self.sems[we], wc)
                            hi[we] = wc
                    fn(engine).then_inc(self.sems[chan], inc)
            getattr(block, e)(section)
        (le, lc) = self.last
        fin = "vector" if le != "vector" else "scalar"
        def _fin(engine):
            engine.wait_ge(self.sems[le], lc)
        getattr(block, fin)(_fin)
        blk.__exit__(None, None, None)


def emit_layernorm(p, src_ap, KT, T, xtmp, sq, rowbuf, bcast, xb, ps, ones_col, ones_row, eps_col):
    TB = cdiv(T, 512)
    assert T % 512 == 0
    for kt in range(KT):
        p.emit("sync", lambda e, kt=kt: e.dma_start(out=xtmp[:], in_=src_ap[kt * 128:(kt + 1) * 128, :]), dma=True)
        p.emit("scalar", lambda e: e.activation(sq[:], xtmp[:], AF.Square))
        for tb in range(TB):
            p.emit("tensor", lambda e, tb=tb, kt=kt: e.matmul(
                ps[0:1, tb, :], ones_col[:, 0:1], xtmp[:, tb * 512:(tb + 1) * 512],
                start=(kt == 0), stop=(kt == KT - 1)))
            p.emit("tensor", lambda e, tb=tb, kt=kt: e.matmul(
                ps[0:1, TB + tb, :], ones_col[:, 0:1], sq[:, tb * 512:(tb + 1) * 512],
                start=(kt == 0), stop=(kt == KT - 1)))
    ND = KT * 128
    for tb in range(TB):
        p.emit("scalar", lambda e, tb=tb: e.mul(rowbuf[0:1, tb * 512:(tb + 1) * 512], ps[0:1, tb, :], 1.0 / ND))
        p.emit("scalar", lambda e, tb=tb: e.mul(rowbuf[0:1, T + tb * 512:T + (tb + 1) * 512], ps[0:1, TB + tb, :], 1.0 / ND))
    mean = rowbuf[0:1, 0:T]
    var = rowbuf[0:1, T:2 * T]
    scr = rowbuf[0:1, 2 * T:3 * T]
    p.emit("vector", lambda e: e.tensor_mul(scr, mean, mean))
    p.emit("vector", lambda e: e.tensor_sub(var, var, scr))
    p.emit("scalar", lambda e: e.activation(var, var, AF.Ln, bias=eps_col[0:1, 0:1]))
    p.emit("scalar", lambda e: e.activation(var, var, AF.Exp, scale=-0.5))
    p.emit("scalar", lambda e: e.mul(mean, mean, -1.0))
    for tb in range(TB):
        p.emit("tensor", lambda e, tb=tb: e.matmul(ps[:, 0, :], ones_row[0:1, 0:128], rowbuf[0:1, tb * 512:(tb + 1) * 512], start=True, stop=True))
        p.emit("scalar", lambda e, tb=tb: e.copy(out=bcast[:, tb * 512:(tb + 1) * 512], in_=ps[:, 0, :]))
        p.emit("tensor", lambda e, tb=tb: e.matmul(ps[:, 1, :], ones_row[0:1, 0:128], rowbuf[0:1, T + tb * 512:T + (tb + 1) * 512], start=True, stop=True))
        p.emit("scalar", lambda e, tb=tb: e.copy(out=bcast[:, T + tb * 512:T + (tb + 1) * 512], in_=ps[:, 1, :]))
    negmB = bcast[:, 0:T]
    rstdB = bcast[:, T:2 * T]
    for kt in range(KT):
        p.emit("sync", lambda e, kt=kt: e.dma_start(out=xtmp[:], in_=src_ap[kt * 128:(kt + 1) * 128, :]), dma=True)
        p.emit("vector", lambda e: e.tensor_add(sq[:], xtmp[:], negmB))
        p.emit("vector", lambda e, kt=kt: e.tensor_mul(xb[:, kt, :], sq[:], rstdB))
    return p.last


def build_a():
    nc = bass.Bass("TRN2", target_bir_lowering=False)
    T = 1024
    xT = nc.dram_tensor("xT", [D, T], F32, kind="ExternalInput").ap()
    w_inT = nc.dram_tensor("w_inT", [D, PROJ], F32, kind="ExternalInput").ap()
    zxbc = nc.dram_tensor("zxbc", [PROJ - NH, T], BF16, kind="ExternalOutput").ap()
    dtout = nc.dram_tensor("dtout", [NH, T], F32, kind="ExternalOutput").ap()
    KT = D // 128
    OT = PROJ // 128
    _st = ExitStack()
    xb = _st.enter_context(nc.sbuf_tensor("xb", [128, KT, T], BF16))
    xtmp = _st.enter_context(nc.sbuf_tensor("xtmp", [128, T], F32))
    sq = _st.enter_context(nc.sbuf_tensor("sq", [128, T], F32))
    rowbuf = _st.enter_context(nc.sbuf_tensor("rowbuf", [1, 3 * T], F32))
    bcast = _st.enter_context(nc.sbuf_tensor("bcast", [128, 2 * T], F32))
    wt = _st.enter_context(nc.sbuf_tensor("wt", [128, 2, 512], F32))
    wb = _st.enter_context(nc.sbuf_tensor("wb", [128, 2, 512], BF16))
    res = _st.enter_context(nc.sbuf_tensor("res", [128, 512], BF16))
    resf = _st.enter_context(nc.sbuf_tensor("resf", [128, 512], F32))
    ones_col = _st.enter_context(nc.sbuf_tensor("ones_col", [128, 1], F32))
    ones_row = _st.enter_context(nc.sbuf_tensor("ones_row", [1, 128], F32))
    eps_col = _st.enter_context(nc.sbuf_tensor("eps_col", [1, 1], F32))
    ps = _st.enter_context(nc.psum_tensor("ps", [128, 8, 512], F32))
    if True:
        p = Prog(nc)
        p.emit("vector", lambda e: e.memset(ones_col[:], 1.0))
        p.emit("vector", lambda e: e.memset(ones_row[:], 1.0))
        p.emit("vector", lambda e: e.memset(eps_col[:], EPS))
        ln_done = emit_layernorm(p, xT, KT, T, xtmp, sq, rowbuf, bcast, xb, ps, ones_col, ones_row, eps_col)
        casts, mml = {}, {}
        evh = ln_done
        k_lin = 0
        for og in range(cdiv(OT, 4)):
            o0 = og * 4
            width = min(4, OT - o0)
            wcols = width * 128
            for kt in range(KT):
                buf = k_lin % 2
                dh = p.emit("sync" if buf == 0 else "gpsimd", lambda e, kt=kt, buf=buf, o0=o0, wcols=wcols: e.dma_start(
                    out=wt[:, buf, 0:wcols], in_=w_inT[kt * 128:(kt + 1) * 128, o0 * 128:o0 * 128 + wcols]),
                    dma=True, deps=[casts.get(k_lin - 2)], chan="wA" if buf == 0 else "wB")
                ch = p.emit("vector", lambda e, buf=buf, wcols=wcols: e.tensor_copy(out=wb[:, buf, 0:wcols], in_=wt[:, buf, 0:wcols]),
                            deps=[dh, mml.get(k_lin - 2)])
                casts[k_lin] = ch
                for oi in range(width):
                    for tb in range(2):
                        deps = [ch, evh, ln_done] if kt == 0 else [ch]
                        h = p.emit("tensor", lambda e, buf=buf, oi=oi, tb=tb, kt=kt: e.matmul(
                            ps[:, oi * 2 + tb, :], wb[:, buf, oi * 128:(oi + 1) * 128],
                            xb[:, kt, tb * 512:(tb + 1) * 512],
                            start=(kt == 0), stop=(kt == KT - 1)), deps=deps)
                mml[k_lin] = h
                k_lin += 1
            first = True
            for oi in range(width):
                o = o0 + oi
                for tb in range(2):
                    dd = [mml[k_lin - 1]] if first else "prev"
                    first = False
                    if o < OT - 1:
                        p.emit("scalar", lambda e, oi=oi, tb=tb: e.copy(out=res[:], in_=ps[:, oi * 2 + tb, :]), deps=dd)
                        p.emit("gpsimd", lambda e, o=o, tb=tb: e.dma_start(
                            out=zxbc[o * 128:(o + 1) * 128, tb * 512:(tb + 1) * 512], in_=res[:]), dma=True)
                    else:
                        p.emit("scalar", lambda e, oi=oi, tb=tb: e.copy(out=resf[:], in_=ps[:, oi * 2 + tb, :]), deps=dd)
                        p.emit("gpsimd", lambda e, tb=tb: e.dma_start(
                            out=dtout[:, tb * 512:(tb + 1) * 512], in_=resf[:]), dma=True)
            evh = p.last
        p.build()
        _st.close()
    return nc


def _emit_dequant(p, qw_ap, sc_ap, qz_ap, out_ap, KT, Ncols, bufs):
    (qt, qzt, zf, zi, zb, srow, sb, shi, shf, df, wbf, ones_row, ps) = bufs
    NB = cdiv(Ncols, 512)
    out_r = out_ap.rearrange("(a b c) n -> a b c n", b=16, c=8)
    for kt in range(KT):
        p.emit("sync", lambda e, kt=kt: e.dma_start(out=qt[0:16, 0:Ncols], in_=qw_ap[kt * 16:(kt + 1) * 16, :]), dma=True)
        p.emit("sync", lambda e, kt=kt: e.dma_start(out=qzt[0:1, 0:Ncols // 8], in_=qz_ap[kt:kt + 1, :]), dma=True)
        p.emit("sync", lambda e, kt=kt: e.dma_start(out=srow[0:1, 0:Ncols], in_=sc_ap[kt:kt + 1, :]), dma=True)
        for j in range(8):
            p.emit("vector", lambda e, j=j: e.tensor_scalar(
                zi[0:1, 0:Ncols // 8, j], qzt[0:1, 0:Ncols // 8], 4 * j, 15,
                op0=ALU.logical_shift_right, op1=ALU.bitwise_and))
        p.emit("vector", lambda e: e.tensor_copy(
            out=zf[0:1, 0:Ncols // 8, :].rearrange("a b c -> a (b c)"),
            in_=zi[0:1, 0:Ncols // 8, :].rearrange("a b c -> a (b c)")))
        zflat = zf[0:1, 0:Ncols // 8, :].rearrange("a b c -> a (b c)")
        p.emit("vector", lambda e: e.tensor_copy(out=zb[0:1, 0:Ncols], in_=zflat))
        p.emit("vector", lambda e: e.tensor_scalar_add(zb[0:1, 0:Ncols], zb[0:1, 0:Ncols], 1.0))
        for nb in range(NB):
            n0, n1 = nb * 512, min(Ncols, (nb + 1) * 512)
            p.emit("tensor", lambda e, n0=n0, n1=n1: e.matmul(
                ps[0:16, 0, 0:n1 - n0], ones_row[0:1, 0:16], zb[0:1, n0:n1], start=True, stop=True))
            p.emit("scalar", lambda e, n0=n0, n1=n1: e.copy(out=zb[0:16, n0:n1], in_=ps[0:16, 0, 0:n1 - n0]))
            p.emit("tensor", lambda e, n0=n0, n1=n1: e.matmul(
                ps[0:16, 1, 0:n1 - n0], ones_row[0:1, 0:16], srow[0:1, n0:n1], start=True, stop=True))
            p.emit("scalar", lambda e, n0=n0, n1=n1: e.copy(out=sb[0:16, n0:n1], in_=ps[0:16, 1, 0:n1 - n0]))
        for j in range(8):
            p.emit("vector", lambda e, j=j: e.tensor_scalar(
                shi[0:16, 0:Ncols], qt[0:16, 0:Ncols], 4 * j, 15,
                op0=ALU.logical_shift_right, op1=ALU.bitwise_and))
            p.emit("vector", lambda e: e.tensor_copy(out=shf[0:16, 0:Ncols], in_=shi[0:16, 0:Ncols]))
            p.emit("vector", lambda e: e.tensor_sub(df[0:16, 0:Ncols], shf[0:16, 0:Ncols], zb[0:16, 0:Ncols]))
            p.emit("vector", lambda e: e.tensor_mul(wbf[0:16, 0:Ncols], df[0:16, 0:Ncols], sb[0:16, 0:Ncols]))
            p.emit("gpsimd", lambda e, kt=kt, j=j: e.dma_start(out=out_r[kt, :, j, :], in_=wbf[0:16, 0:Ncols]), dma=True)


def build_f1():
    nc = bass.Bass("TRN2", target_bir_lowering=False)
    NG = INTER // 8
    ND = D // 8
    gqw = nc.dram_tensor("gqw", [512, NG], I32, kind="ExternalInput").ap()
    gsc = nc.dram_tensor("gsc", [32, NG], F32, kind="ExternalInput").ap()
    gqz = nc.dram_tensor("gqz", [32, NG // 8], I32, kind="ExternalInput").ap()
    uqw = nc.dram_tensor("uqw", [512, NG], I32, kind="ExternalInput").ap()
    usc = nc.dram_tensor("usc", [32, NG], F32, kind="ExternalInput").ap()
    uqz = nc.dram_tensor("uqz", [32, NG // 8], I32, kind="ExternalInput").ap()
    dqw = nc.dram_tensor("dqw", [INTER // 8, ND], I32, kind="ExternalInput").ap()
    dsc = nc.dram_tensor("dsc", [86, ND], F32, kind="ExternalInput").ap()
    dqz = nc.dram_tensor("dqz", [86, ND // 8], I32, kind="ExternalInput").ap()
    gdq = nc.dram_tensor("gdq", [D, NG], BF16, kind="ExternalOutput").ap()
    udq = nc.dram_tensor("udq", [D, NG], BF16, kind="ExternalOutput").ap()
    ddq = nc.dram_tensor("ddq", [INTER, ND], BF16, kind="ExternalOutput").ap()
    NMAX = NG
    _st = ExitStack()
    qt = _st.enter_context(nc.sbuf_tensor("qt", [16, NMAX], I32))
    qzt = _st.enter_context(nc.sbuf_tensor("qzt", [1, NMAX // 8], I32))
    zf = _st.enter_context(nc.sbuf_tensor("zf", [1, NMAX // 8, 8], F32))
    zi = _st.enter_context(nc.sbuf_tensor("zi", [1, NMAX // 8, 8], I32))
    zb = _st.enter_context(nc.sbuf_tensor("zb", [16, NMAX], F32))
    srow = _st.enter_context(nc.sbuf_tensor("srow", [1, NMAX], F32))
    sb = _st.enter_context(nc.sbuf_tensor("sb", [16, NMAX], F32))
    shi = _st.enter_context(nc.sbuf_tensor("shi", [16, NMAX], I32))
    shf = _st.enter_context(nc.sbuf_tensor("shf", [16, NMAX], F32))
    df = _st.enter_context(nc.sbuf_tensor("df", [16, NMAX], F32))
    wbf = _st.enter_context(nc.sbuf_tensor("wbf", [16, NMAX], BF16))
    ones_row = _st.enter_context(nc.sbuf_tensor("ones_row", [1, 16], F32))
    ps = _st.enter_context(nc.psum_tensor("ps", [128, 2, 512], F32))
    if True:
        p = Prog(nc)
        p.emit("vector", lambda e: e.memset(ones_row[:], 1.0))
        bufs = (qt, qzt, zf, zi, zb, srow, sb, shi, shf, df, wbf, ones_row, ps)
        _emit_dequant(p, gqw, gsc, gqz, gdq, 32, NG, bufs)
        _emit_dequant(p, uqw, usc, uqz, udq, 32, NG, bufs)
        _emit_dequant(p, dqw, dsc, dqz, ddq, 86, ND, bufs)
        p.build()
        _st.close()
    return nc


def build_bc():
    nc = bass.Bass("TRN2", target_bir_lowering=False)
    U = 4
    gatei = nc.dram_tensor("gatei", [U, 1024, L], BF16, kind="ExternalInput").ap()
    xbci = nc.dram_tensor("xbci", [U, 1280, L], BF16, kind="ExternalInput").ap()
    dti = nc.dram_tensor("dti", [U, 16, L], F32, kind="ExternalInput").ap()
    cw = nc.dram_tensor("cw", [U, 1280, 4], F32, kind="ExternalInput").ap()
    cb = nc.dram_tensor("cb", [U, 1280, 1], F32, kind="ExternalInput").ap()
    dtb = nc.dram_tensor("dtb", [U, 16, 1], F32, kind="ExternalInput").ap()
    alog = nc.dram_tensor("alog", [U, 16, 1], F32, kind="ExternalInput").ap()
    drow = nc.dram_tensor("drow", [U, 1, 16], F32, kind="ExternalInput").ap()
    nwi = nc.dram_tensor("nwi", [U, 128, 8], F32, kind="ExternalInput").ap()
    maskTi = nc.dram_tensor("maskTi", [128, 2, 256], F32, kind="ExternalInput").ap()
    identbi = nc.dram_tensor("identbi", [128, 128], BF16, kind="ExternalInput").ap()
    identfi = nc.dram_tensor("identfi", [128, 128], F32, kind="ExternalInput").ap()
    yn = nc.dram_tensor("yn", [U, 1024, L], BF16, kind="ExternalOutput").ap()
    _st = ExitStack()
    xc = _st.enter_context(nc.sbuf_tensor("xc", [128, 10, L + 3], BF16))
    xbc_c = _st.enter_context(nc.sbuf_tensor("xbc_c", [128, 10, L], BF16))
    scrf = _st.enter_context(nc.sbuf_tensor("scrf", [128, L], F32))
    y_unit = _st.enter_context(nc.sbuf_tensor("y_unit", [128, 8, L], BF16))
    gatet = _st.enter_context(nc.sbuf_tensor("gatet", [128, L], BF16))
    ynt = _st.enter_context(nc.sbuf_tensor("ynt", [128, L], BF16))
    rstdB = _st.enter_context(nc.sbuf_tensor("rstdB", [128, L], F32))
    cwt = _st.enter_context(nc.sbuf_tensor("cwt", [128, 10, 4], F32))
    cbt = _st.enter_context(nc.sbuf_tensor("cbt", [128, 10, 1], F32))
    dtw = _st.enter_context(nc.sbuf_tensor("dtw", [16, 3, L], F32))
    zc16 = _st.enter_context(nc.sbuf_tensor("zc16", [16, CS], F32))
    dtbc = _st.enter_context(nc.sbuf_tensor("dtbc", [16, 1], F32))
    acol = _st.enter_context(nc.sbuf_tensor("acol", [16, 1], F32))
    arow = _st.enter_context(nc.sbuf_tensor("arow", [1, 2, CS], F32))
    eAt = _st.enter_context(nc.sbuf_tensor("eAt", [1, 128], F32))
    eB = _st.enter_context(nc.sbuf_tensor("eB", [128, 128], F32))
    DB = _st.enter_context(nc.sbuf_tensor("DB", [128, 16], F32))
    nw = _st.enter_context(nc.sbuf_tensor("nw", [128, 8], F32))
    dtT = _st.enter_context(nc.sbuf_tensor("dtT", [128, 16, 16], F32))
    GmT_sb = _st.enter_context(nc.sbuf_tensor("GmT_sb", [128, 2, CS], F32))
    LT_sb = _st.enter_context(nc.sbuf_tensor("LT_sb", [128, 2, CS], F32))
    PT_sb = _st.enter_context(nc.sbuf_tensor("PT_sb", [128, 2, CS], BF16))
    BT_sb = _st.enter_context(nc.sbuf_tensor("BT_sb", [128, 2, 128], BF16))
    xsT_sb = _st.enter_context(nc.sbuf_tensor("xsT_sb", [128, 2, 64], BF16))
    xsdT_sb = _st.enter_context(nc.sbuf_tensor("xsdT_sb", [128, 2, 64], BF16))
    Cexp_sb = _st.enter_context(nc.sbuf_tensor("Cexp_sb", [128, CS], BF16))
    Erow = _st.enter_context(nc.sbuf_tensor("Erow", [1, CS], F32))
    tmpf = _st.enter_context(nc.sbuf_tensor("tmpf", [128, CS], F32))
    R_sb = _st.enter_context(nc.sbuf_tensor("R_sb", [128, 16, 64], F32))
    Rbf = _st.enter_context(nc.sbuf_tensor("Rbf", [128, 16, 64], BF16))
    maskT = _st.enter_context(nc.sbuf_tensor("maskT", [128, 2, CS], F32))
    identb = _st.enter_context(nc.sbuf_tensor("identb", [128, 128], BF16))
    identf = _st.enter_context(nc.sbuf_tensor("identf", [128, 128], F32))
    ones_col = _st.enter_context(nc.sbuf_tensor("ones_col", [128, 1], F32))
    ones_row = _st.enter_context(nc.sbuf_tensor("ones_row", [1, 256], F32))
    eps_col = _st.enter_context(nc.sbuf_tensor("eps_col", [1, 1], F32))
    rowb = _st.enter_context(nc.sbuf_tensor("rowb", [1, L], F32))
    drow_sb = _st.enter_context(nc.sbuf_tensor("drow_sb", [1, 16], F32))
    pA = _st.enter_context(nc.psum_tensor("pA", [128, 2, CS], F32))
    pG = _st.enter_context(nc.psum_tensor("pG", [128, 2, CS], F32))
    pYS = _st.enter_context(nc.psum_tensor("pYS", [128, 320], F32))
    pTT = _st.enter_context(nc.psum_tensor("pTT", [128, 2, 192], BF16))
    pbc = _st.enter_context(nc.psum_tensor("pbc", [128, 512], F32))
    ps_r = _st.enter_context(nc.psum_tensor("ps_r", [128, 2, 512], F32))
    ptr = _st.enter_context(nc.psum_tensor("ptr", [128, 16], F32))
    if True:
        p = Prog(nc)
        p.emit("vector", lambda e: e.memset(ones_col[:], 1.0))
        p.emit("vector", lambda e: e.memset(ones_row[:], 1.0))
        p.emit("vector", lambda e: e.memset(eps_col[:], EPS))
        p.emit("vector", lambda e: e.memset(zc16[:], 0.0))
        p.emit("sync", lambda e: e.dma_start(out=maskT[:], in_=maskTi), dma=True)
        p.emit("sync", lambda e: e.dma_start(out=identb[:], in_=identbi), dma=True)
        p.emit("sync", lambda e: e.dma_start(out=identf[:], in_=identfi), dma=True)
        for u in range(U):
            for cht in range(10):
                p.emit("sync", lambda e, u=u, cht=cht: e.dma_start(out=cwt[:, cht, :], in_=cw[u, cht * 128:(cht + 1) * 128, :]), dma=True)
                p.emit("sync", lambda e, u=u, cht=cht: e.dma_start(out=cbt[:, cht, :], in_=cb[u, cht * 128:(cht + 1) * 128, :]), dma=True)
                p.emit("vector", lambda e, cht=cht: e.memset(xc[:, cht, 0:3], 0.0))
                p.emit("sync", lambda e, u=u, cht=cht: e.dma_start(out=xc[:, cht, 3:L + 3], in_=xbci[u, cht * 128:(cht + 1) * 128, :]), dma=True)
                p.emit("vector", lambda e, cht=cht: e.tensor_scalar(scrf[:], xc[:, cht, 0:L], cwt[:, cht, 0:1], None, op0=ALU.mult))
                for j in range(1, 4):
                    p.emit("vector", lambda e, cht=cht, j=j: e.scalar_tensor_tensor(
                        scrf[:], xc[:, cht, j:j + L], cwt[:, cht, j:j + 1], scrf[:], op0=ALU.mult, op1=ALU.add))
                p.emit("scalar", lambda e, cht=cht: e.activation(xbc_c[:, cht, :], scrf[:], AF.Silu, bias=cbt[:, cht, 0:1]))
            p.emit("sync", lambda e, u=u: e.dma_start(out=dtw[0:16, 0, :], in_=dti[u]), dma=True)
            p.emit("sync", lambda e, u=u: e.dma_start(out=dtbc[:], in_=dtb[u]), dma=True)
            p.emit("sync", lambda e, u=u: e.dma_start(out=acol[:], in_=alog[u]), dma=True)
            p.emit("sync", lambda e, u=u: e.dma_start(out=drow_sb[:], in_=drow[u]), dma=True)
            p.emit("sync", lambda e, u=u: e.dma_start(out=nw[:], in_=nwi[u]), dma=True)
            p.emit("vector", lambda e: e.tensor_scalar(dtw[0:16, 0, :], dtw[0:16, 0, :], dtbc[:], None, op0=ALU.add))
            p.emit("scalar", lambda e: e.activation(dtw[0:16, 1, :], dtw[0:16, 0, :], AF.Abs))
            p.emit("scalar", lambda e: e.activation(dtw[0:16, 1, :], dtw[0:16, 1, :], AF.Exp, scale=-1.0))
            p.emit("scalar", lambda e: e.activation(dtw[0:16, 1, :], dtw[0:16, 1, :], AF.Ln, bias=ones_col[0:16, 0:1]))
            p.emit("vector", lambda e: e.tensor_scalar(dtw[0:16, 2, :], dtw[0:16, 0, :], zc16[0:16, 0:1], None, op0=ALU.max))
            p.emit("vector", lambda e: e.tensor_add(dtw[0:16, 0, :], dtw[0:16, 2, :], dtw[0:16, 1, :]))
            p.emit("scalar", lambda e: e.activation(acol[:], acol[:], AF.Exp))
            p.emit("scalar", lambda e: e.mul(acol[:], acol[:], -1.0))
            p.emit("vector", lambda e: e.tensor_scalar(dtw[0:16, 1, :], dtw[0:16, 0, :], acol[:], None, op0=ALU.mult))
            for c in range(NCH):
                p.emit("vector", lambda e, c=c: e.tensor_tensor_scan(
                    dtw[0:16, 2, c * CS:(c + 1) * CS], dtw[0:16, 1, c * CS:(c + 1) * CS], zc16[:],
                    0.0, op0=ALU.add, op1=ALU.add))
            for tt in range(16):
                p.emit("tensor", lambda e, tt=tt: e.transpose(ptr[:, 0:16], dtw[0:16, 0, tt * 128:(tt + 1) * 128], identf[0:16, 0:16]))
                p.emit("vector", lambda e, tt=tt: e.tensor_copy(out=dtT[:, tt, :], in_=ptr[:, 0:16]))
            p.emit("sync", lambda e: e.dma_start(
                out=eAt[0:1, :], in_=dtw[0:16, 2, :].rearrange("p (c t) -> p c t", t=CS)[:, :, 255],
                allow_slow_non_contiguous=True), dma=True)
            p.emit("scalar", lambda e: e.activation(eAt[0:1, :], eAt[0:1, :], AF.Exp))
            p.emit("tensor", lambda e: e.matmul(pbc[:, 0:128], ones_row[0:1, 0:128], eAt[0:1, :], start=True, stop=True))
            p.emit("vector", lambda e: e.tensor_copy(out=eB[:], in_=pbc[:, 0:128]))
            p.emit("tensor", lambda e: e.matmul(pbc[:, 0:16], ones_row[0:1, 0:128], drow_sb[0:1, :], start=True, stop=True))
            p.emit("vector", lambda e: e.tensor_copy(out=DB[:], in_=pbc[:, 0:16]))
            p.emit("vector", lambda e: e.memset(R_sb[:], 0.0))
            p.emit("vector", lambda e: e.memset(Rbf[:], 0.0))
            for c in range(NCH):
                for jj in range(2):
                    p.emit("tensor", lambda e, c=c, jj=jj: e.matmul(
                        pG[:, jj, :], xbc_c[:, 8, c * CS + jj * 128:c * CS + (jj + 1) * 128],
                        xbc_c[:, 9, c * CS:(c + 1) * CS], start=True, stop=True))
                    p.emit("vector", lambda e, jj=jj: e.tensor_copy(out=GmT_sb[:, jj, :], in_=pG[:, jj, :]))
                    p.emit("tensor", lambda e, c=c, jj=jj: e.transpose(
                        pTT[:, jj, 0:128], xbc_c[:, 8, c * CS + jj * 128:c * CS + (jj + 1) * 128], identb[:]))
                    p.emit("vector", lambda e, jj=jj: e.tensor_copy(out=BT_sb[:, jj, :], in_=pTT[:, jj, 0:128]))
                for h in range(16):
                    hc = h * 8 + c
                    po = 64 * (h % 2)
                    xh = lambda jj, h=h, c=c: xbc_c[64 * (h % 2):64 * (h % 2) + 64, h // 2, c * CS + jj * 128:c * CS + (jj + 1) * 128]
                    p.emit("sync", lambda e, h=h, c=c: e.dma_start(
                        out=arow[0:1, 0, :], in_=dtw[h:h + 1, 2, c * CS:(c + 1) * CS]), dma=True)
                    p.emit("scalar", lambda e: e.mul(arow[0:1, 1, :], arow[0:1, 0, :], -1.0))
                    for jj in range(2):
                        p.emit("tensor", lambda e, jj=jj: e.matmul(
                            pA[:, jj, :], arow[0:1, 1, jj * 128:(jj + 1) * 128], ones_row[0:1, 0:256], start=True, stop=False))
                        p.emit("tensor", lambda e, jj=jj: e.matmul(
                            pA[:, jj, :], ones_row[0:1, 0:128], arow[0:1, 0, :], start=False, stop=True))
                        p.emit("vector", lambda e, jj=jj: e.tensor_add(tmpf[:], pA[:, jj, :], maskT[:, jj, :]))
                        p.emit("scalar", lambda e, jj=jj: e.activation(LT_sb[:, jj, :], tmpf[:], AF.Exp))
                        p.emit("vector", lambda e, jj=jj: e.tensor_mul(PT_sb[:, jj, :], LT_sb[:, jj, :], GmT_sb[:, jj, :]))
                        p.emit("tensor", lambda e, jj=jj, xh=xh, po=po: e.transpose(pTT[:, jj, 128:192], xh(jj), identb[po:po + 64, po:po + 64]))
                        p.emit("vector", lambda e, jj=jj, c=c, h=h: e.tensor_scalar(
                            xsT_sb[:, jj, :], pTT[:, jj, 128:192], dtT[:, 2 * c + jj, h:h + 1], None, op0=ALU.mult))
                        p.emit("vector", lambda e, jj=jj: e.tensor_scalar(
                            xsdT_sb[:, jj, :], xsT_sb[:, jj, :], LT_sb[:, jj, 255:256], None, op0=ALU.mult))
                    p.emit("scalar", lambda e: e.activation(Erow[0:1, :], arow[0:1, 0, :], AF.Exp))
                    p.emit("tensor", lambda e: e.matmul(pbc[:, 0:256], ones_row[0:1, 0:128], Erow[0:1, :], start=True, stop=True))
                    p.emit("vector", lambda e, c=c: e.tensor_mul(Cexp_sb[:], xbc_c[:, 9, c * CS:(c + 1) * CS], pbc[:, 0:256]))
                    p.emit("tensor", lambda e, po=po: e.matmul(pYS[po:po + 64, 0:256], xsT_sb[:, 0, :], PT_sb[:, 0, :], start=True, stop=False))
                    p.emit("tensor", lambda e, po=po: e.matmul(pYS[po:po + 64, 0:256], xsT_sb[:, 1, :], PT_sb[:, 1, :], start=False, stop=False))
                    p.emit("tensor", lambda e, po=po, h=h: e.matmul(pYS[po:po + 64, 0:256], Rbf[:, h, :], Cexp_sb[:], start=False, stop=True))
                    p.emit("tensor", lambda e: e.matmul(pYS[:, 256:320], BT_sb[:, 0, :], xsdT_sb[:, 0, :], start=True, stop=False))
                    p.emit("tensor", lambda e: e.matmul(pYS[:, 256:320], BT_sb[:, 1, :], xsdT_sb[:, 1, :], start=False, stop=True))
                    p.emit("vector", lambda e, h=h, hc=hc: e.scalar_tensor_tensor(
                        R_sb[:, h, :], R_sb[:, h, :], eB[:, hc:hc + 1], pYS[:, 256:320], op0=ALU.mult, op1=ALU.add))
                    p.emit("vector", lambda e, h=h: e.tensor_copy(out=Rbf[:, h, :], in_=R_sb[:, h, :]))
                    p.emit("vector", lambda e, h=h, c=c, po=po: e.scalar_tensor_tensor(
                        y_unit[po:po + 64, h // 2, c * CS:(c + 1) * CS],
                        xbc_c[po:po + 64, h // 2, c * CS:(c + 1) * CS],
                        DB[po:po + 64, h:h + 1], pYS[po:po + 64, 0:256], op0=ALU.mult, op1=ALU.add))
            for cht in range(8):
                p.emit("sync", lambda e, u=u, cht=cht: e.dma_start(out=gatet[:], in_=gatei[u, cht * 128:(cht + 1) * 128, :]), dma=True)
                p.emit("scalar", lambda e: e.activation(scrf[:], gatet[:], AF.Silu))
                p.emit("vector", lambda e, cht=cht: e.tensor_mul(y_unit[:, cht, :], y_unit[:, cht, :], scrf[:]))
            for half in range(2):
                for cht in range(8):
                    p.emit("scalar", lambda e, cht=cht, half=half: e.activation(
                        scrf[:, 0:1024], y_unit[:, cht, half * 1024:(half + 1) * 1024], AF.Square))
                    for tbh in range(2):
                        p.emit("tensor", lambda e, tbh=tbh, cht=cht: e.matmul(
                            ps_r[0:1, tbh, :], ones_col[:, 0:1], scrf[:, tbh * 512:(tbh + 1) * 512],
                            start=(cht == 0), stop=(cht == 7)))
                for tbh in range(2):
                    p.emit("scalar", lambda e, half=half, tbh=tbh: e.mul(
                        rowb[0:1, half * 1024 + tbh * 512:half * 1024 + (tbh + 1) * 512], ps_r[0:1, tbh, :], 1.0 / 1024))
            p.emit("scalar", lambda e: e.activation(rowb[0:1, :], rowb[0:1, :], AF.Ln, bias=eps_col[0:1, 0:1]))
            p.emit("scalar", lambda e: e.activation(rowb[0:1, :], rowb[0:1, :], AF.Exp, scale=-0.5))
            for tb in range(4):
                p.emit("tensor", lambda e, tb=tb: e.matmul(pbc[:, 0:512], ones_row[0:1, 0:128], rowb[0:1, tb * 512:(tb + 1) * 512], start=True, stop=True))
                p.emit("vector", lambda e, tb=tb: e.tensor_copy(out=rstdB[:, tb * 512:(tb + 1) * 512], in_=pbc[:, 0:512]))
            for cht in range(8):
                p.emit("vector", lambda e, cht=cht: e.scalar_tensor_tensor(
                    ynt[:], y_unit[:, cht, :], nw[:, cht:cht + 1], rstdB[:], op0=ALU.mult, op1=ALU.mult))
                p.emit("gpsimd", lambda e, u=u, cht=cht: e.dma_start(out=yn[u, cht * 128:(cht + 1) * 128, :], in_=ynt[:]), dma=True)
        p.build()
        _st.close()
    return nc


def build_d1():
    nc = bass.Bass("TRN2", target_bir_lowering=False)
    T = 1024
    ynf = nc.dram_tensor("ynf", [DIN, T], BF16, kind="ExternalInput").ap()
    xres = nc.dram_tensor("xres", [D, T], F32, kind="ExternalInput").ap()
    opT = nc.dram_tensor("opT", [DIN, D], F32, kind="ExternalInput").ap()
    r1 = nc.dram_tensor("r1", [D, T], F32, kind="ExternalOutput").ap()
    KT = DIN // 128
    OT = D // 128
    _st = ExitStack()
    ynh = _st.enter_context(nc.sbuf_tensor("ynh", [128, KT, 512], BF16))
    wt = _st.enter_context(nc.sbuf_tensor("wt", [128, 2, 1024], F32))
    wb = _st.enter_context(nc.sbuf_tensor("wb", [128, 2, 1024], BF16))
    xrt = _st.enter_context(nc.sbuf_tensor("xrt", [128, 512], F32))
    outf = _st.enter_context(nc.sbuf_tensor("outf", [128, 512], F32))
    ps = _st.enter_context(nc.psum_tensor("ps", [128, 8, 512], F32))
    if True:
        p = Prog(nc)
        casts, mml = {}, {}
        k_lin = 0
        evh = None
        for tb in range(2):
            prev_mm = mml.get(k_lin - 1)
            yh = None
            for kt in range(KT):
                yh = p.emit("sync", lambda e, kt=kt, tb=tb: e.dma_start(
                    out=ynh[:, kt, :], in_=ynf[kt * 128:(kt + 1) * 128, tb * 512:(tb + 1) * 512]),
                    dma=True, deps=[prev_mm])
            for og in range(4):
                for kt in range(KT):
                    buf = k_lin % 2
                    dh = p.emit("sync" if buf == 0 else "gpsimd", lambda e, kt=kt, og=og, buf=buf: e.dma_start(
                        out=wt[:, buf, :], in_=opT[kt * 128:(kt + 1) * 128, og * 1024:(og + 1) * 1024]),
                        dma=True, deps=[casts.get(k_lin - 2)], chan="wA" if buf == 0 else "wB")
                    ch = p.emit("vector", lambda e, buf=buf: e.tensor_copy(out=wb[:, buf, :], in_=wt[:, buf, :]),
                                deps=[dh, mml.get(k_lin - 2)])
                    casts[k_lin] = ch
                    for oi in range(8):
                        deps = [ch, yh, evh] if kt == 0 else [ch]
                        h = p.emit("tensor", lambda e, buf=buf, oi=oi, kt=kt: e.matmul(
                            ps[:, oi, :], wb[:, buf, oi * 128:(oi + 1) * 128], ynh[:, kt, :],
                            start=(kt == 0), stop=(kt == KT - 1)), deps=deps)
                    mml[k_lin] = h
                    k_lin += 1
                first = True
                for oi in range(8):
                    o = og * 8 + oi
                    p.emit("sync", lambda e, o=o, tb=tb: e.dma_start(
                        out=xrt[:], in_=xres[o * 128:(o + 1) * 128, tb * 512:(tb + 1) * 512]),
                        dma=True, deps=([mml[k_lin - 1]] if first else "prev"))
                    first = False
                    p.emit("vector", lambda e, oi=oi: e.tensor_add(outf[:], ps[:, oi, :], xrt[:]))
                    p.emit("gpsimd", lambda e, o=o, tb=tb: e.dma_start(
                        out=r1[o * 128:(o + 1) * 128, tb * 512:(tb + 1) * 512], in_=outf[:]), dma=True)
                evh = p.last
        p.build()
        _st.close()
    return nc


def build_d2():
    nc = bass.Bass("TRN2", target_bir_lowering=False)
    T = 1024
    r1i = nc.dram_tensor("r1i", [D, T], F32, kind="ExternalInput").ap()
    fWT = nc.dram_tensor("fWT", [D, D], F32, kind="ExternalInput").ap()
    laT = nc.dram_tensor("laT", [D, 128], F32, kind="ExternalInput").ap()
    lbT = nc.dram_tensor("lbT", [128, D], F32, kind="ExternalInput").ap()
    fbias = nc.dram_tensor("fbias", [128, 32], F32, kind="ExternalInput").ap()
    lw = nc.dram_tensor("lw", [8, 1], F32, kind="ExternalInput").ap()
    e8 = nc.dram_tensor("e8", [8, 128], F32, kind="ExternalInput").ap()
    wqi = nc.dram_tensor("wqi", [D, D], F32, kind="ExternalInput").ap()
    wki = nc.dram_tensor("wki", [D, 1024], F32, kind="ExternalInput").ap()
    wvi = nc.dram_tensor("wvi", [D, 1024], F32, kind="ExternalInput").ap()
    c2q = nc.dram_tensor("c2q", [128, T], F32, kind="ExternalInput").ap()
    s2q = nc.dram_tensor("s2q", [128, T], F32, kind="ExternalInput").ap()
    c2k = nc.dram_tensor("c2k", [128, T], F32, kind="ExternalInput").ap()
    s2k = nc.dram_tensor("s2k", [128, T], F32, kind="ExternalInput").ap()
    pswi = nc.dram_tensor("pswi", [128, 128], F32, kind="ExternalInput").ap()
    r2o = nc.dram_tensor("r2o", [D, T], F32, kind="ExternalOutput").ap()
    r2b = nc.dram_tensor("r2b", [D, T], F32, kind="Internal").ap()
    qbf = nc.dram_tensor("qbf", [D, T], BF16, kind="ExternalOutput").ap()
    kbf = nc.dram_tensor("kbf", [1024, T], BF16, kind="ExternalOutput").ap()
    vbf = nc.dram_tensor("vbf", [1024, T], BF16, kind="ExternalOutput").ap()
    KT = D // 128
    _st = ExitStack()
    xb = _st.enter_context(nc.sbuf_tensor("xb", [128, KT, T], BF16))
    xtmp = _st.enter_context(nc.sbuf_tensor("xtmp", [128, T], F32))
    sq = _st.enter_context(nc.sbuf_tensor("sq", [128, T], F32))
    rowbuf = _st.enter_context(nc.sbuf_tensor("rowbuf", [1, 3 * T], F32))
    bcast = _st.enter_context(nc.sbuf_tensor("bcast", [128, 2 * T], F32))
    wt = _st.enter_context(nc.sbuf_tensor("wt", [128, 2, 512], F32))
    wb = _st.enter_context(nc.sbuf_tensor("wb", [128, 2, 512], BF16))
    hbf = _st.enter_context(nc.sbuf_tensor("hbf", [128, 2, 512], BF16))
    coefcol = _st.enter_context(nc.sbuf_tensor("coefcol", [128, 1], F32))
    e8t = _st.enter_context(nc.sbuf_tensor("e8t", [8, 128], F32))
    lwt = _st.enter_context(nc.sbuf_tensor("lwt", [8, 1], F32))
    fbias_t = _st.enter_context(nc.sbuf_tensor("fbias_t", [128, 32], F32))
    laTt = _st.enter_context(nc.sbuf_tensor("laTt", [128, KT, 128], F32))
    laTb = _st.enter_context(nc.sbuf_tensor("laTb", [128, KT, 128], BF16))
    lbTt = _st.enter_context(nc.sbuf_tensor("lbTt", [128, D], F32))
    tabs = _st.enter_context(nc.sbuf_tensor("tabs", [128, 4, T], F32))
    psw = _st.enter_context(nc.sbuf_tensor("psw", [128, 128], F32))
    hf = _st.enter_context(nc.sbuf_tensor("hf", [128, 512], F32))
    tf1 = _st.enter_context(nc.sbuf_tensor("tf1", [128, 512], F32))
    tf2 = _st.enter_context(nc.sbuf_tensor("tf2", [128, 512], F32))
    obf = _st.enter_context(nc.sbuf_tensor("obf", [128, 512], BF16))
    outf = _st.enter_context(nc.sbuf_tensor("outf", [128, 512], F32))
    r1t = _st.enter_context(nc.sbuf_tensor("r1t", [128, 512], F32))
    ones_col = _st.enter_context(nc.sbuf_tensor("ones_col", [128, 1], F32))
    ones_row = _st.enter_context(nc.sbuf_tensor("ones_row", [1, 128], F32))
    eps_col = _st.enter_context(nc.sbuf_tensor("eps_col", [1, 1], F32))
    ps = _st.enter_context(nc.psum_tensor("ps", [128, 8, 512], F32))
    if True:
        p = Prog(nc)
        p.emit("vector", lambda e: e.memset(ones_col[:], 1.0))
        p.emit("vector", lambda e: e.memset(ones_row[:], 1.0))
        p.emit("vector", lambda e: e.memset(eps_col[:], EPS))
        p.emit("sync", lambda e: e.dma_start(out=e8t[:], in_=e8), dma=True)
        p.emit("sync", lambda e: e.dma_start(out=lwt[:], in_=lw), dma=True)
        p.emit("sync", lambda e: e.dma_start(out=fbias_t[:], in_=fbias), dma=True)
        p.emit("sync", lambda e: e.dma_start(out=lbTt[:], in_=lbT), dma=True)
        p.emit("sync", lambda e: e.dma_start(out=psw[:], in_=pswi), dma=True)
        for kt in range(KT):
            p.emit("sync", lambda e, kt=kt: e.dma_start(out=laTt[:, kt, :], in_=laT[kt * 128:(kt + 1) * 128, :]), dma=True)
            p.emit("vector", lambda e, kt=kt: e.tensor_copy(out=laTb[:, kt, :], in_=laTt[:, kt, :]))
        for i, t in enumerate([c2q, s2q, c2k, s2k]):
            p.emit("sync", lambda e, i=i, t=t: e.dma_start(out=tabs[:, i, :], in_=t), dma=True)
        emit_layernorm(p, r1i, KT, T, xtmp, sq, rowbuf, bcast, xb, ps, ones_col, ones_row, eps_col)
        p.emit("tensor", lambda e: e.matmul(ps[:, 0, 0:1], e8t[:], lwt[:], start=True, stop=True))
        p.emit("scalar", lambda e: e.mul(coefcol[:], ps[:, 0, 0:1], 2.0))
        for tb in range(2):
            for kt in range(KT):
                p.emit("tensor", lambda e, kt=kt, tb=tb: e.matmul(
                    ps[:, 2 + tb, :], laTb[:, kt, :], xb[:, kt, tb * 512:(tb + 1) * 512],
                    start=(kt == 0), stop=(kt == KT - 1)))
            p.emit("vector", lambda e, tb=tb: e.tensor_scalar(
                hbf[:, tb, :], ps[:, 2 + tb, :], coefcol[:], None, op0=ALU.mult))
        casts, mml = {}, {}
        k_lin = 0
        evh = p.last
        for og in range(8):
            for kt in range(KT):
                buf = k_lin % 2
                extra = [evh] if kt < 2 else []
                dh = p.emit("sync" if buf == 0 else "gpsimd", lambda e, kt=kt, og=og, buf=buf: e.dma_start(
                    out=wt[:, buf, :], in_=fWT[kt * 128:(kt + 1) * 128, og * 512:(og + 1) * 512]),
                    dma=True, deps=[casts.get(k_lin - 2)] + extra, chan="wA" if buf == 0 else "wB")
                ch = p.emit("vector", lambda e, buf=buf: e.tensor_copy(out=wb[:, buf, :], in_=wt[:, buf, :]),
                            deps=[dh, mml.get(k_lin - 2)] + extra)
                casts[k_lin] = ch
                for oi in range(4):
                    for tb in range(2):
                        deps = [ch, evh] if kt == 0 else [ch]
                        h = p.emit("tensor", lambda e, buf=buf, oi=oi, tb=tb, kt=kt: e.matmul(
                            ps[:, oi * 2 + tb, :], wb[:, buf, oi * 128:(oi + 1) * 128],
                            xb[:, kt, tb * 512:(tb + 1) * 512], start=(kt == 0), stop=False), deps=deps)
                mml[k_lin] = h
                k_lin += 1
            for oi in range(4):
                o = og * 4 + oi
                p.emit("vector", lambda e, o=o: e.tensor_copy(out=wt[:, 0, 0:128], in_=lbTt[:, o * 128:(o + 1) * 128]))
                p.emit("vector", lambda e: e.tensor_copy(out=wb[:, 0, 0:128], in_=wt[:, 0, 0:128]))
                for tb in range(2):
                    p.emit("tensor", lambda e, oi=oi, tb=tb: e.matmul(
                        ps[:, oi * 2 + tb, :], wb[:, 0, 0:128], hbf[:, tb, :], start=False, stop=True))
                    p.emit("vector", lambda e, oi=oi, tb=tb, o=o: e.tensor_scalar(
                        outf[:], ps[:, oi * 2 + tb, :], fbias_t[:, o:o + 1], None, op0=ALU.add))
                    p.emit("sync", lambda e, o=o, tb=tb: e.dma_start(
                        out=r1t[:], in_=r1i[o * 128:(o + 1) * 128, tb * 512:(tb + 1) * 512]), dma=True)
                    p.emit("vector", lambda e: e.tensor_add(outf[:], outf[:], r1t[:]))
                    p.emit("gpsimd", lambda e, o=o, tb=tb: e.dma_start(
                        out=r2o[o * 128:(o + 1) * 128, tb * 512:(tb + 1) * 512], in_=outf[:]), dma=True)
                    p.emit("gpsimd", lambda e, o=o, tb=tb: e.dma_start(
                        out=r2b[o * 128:(o + 1) * 128, tb * 512:(tb + 1) * 512], in_=outf[:]), dma=True)
            evh = p.last
        emit_layernorm(p, r2b, KT, T, xtmp, sq, rowbuf, bcast, xb, ps, ones_col, ones_row, eps_col)
        qstate = {"k_lin": 0, "evh": p.last, "casts": {}, "mml": {}}
        def qkv(w_ap, OT, out_ap, rope, qtab):
            casts, mml = qstate["casts"], qstate["mml"]
            for og in range(cdiv(OT, 4)):
                width = min(4, OT - og * 4)
                for kt in range(KT):
                    k_lin = qstate["k_lin"]
                    buf = k_lin % 2
                    dh = p.emit("sync" if buf == 0 else "gpsimd", lambda e, kt=kt, og=og, buf=buf, width=width, w_ap=w_ap: e.dma_start(
                        out=wt[:, buf, 0:width * 128], in_=w_ap[kt * 128:(kt + 1) * 128, og * 512:og * 512 + width * 128]),
                        dma=True, deps=[casts.get(k_lin - 2)], chan="wA" if buf == 0 else "wB")
                    ch = p.emit("vector", lambda e, buf=buf, width=width: e.tensor_copy(out=wb[:, buf, 0:width * 128], in_=wt[:, buf, 0:width * 128]),
                                deps=[dh, mml.get(k_lin - 2)])
                    casts[k_lin] = ch
                    for oi in range(width):
                        for tb in range(2):
                            deps = [ch, qstate["evh"]] if kt == 0 else [ch]
                            h = p.emit("tensor", lambda e, buf=buf, oi=oi, tb=tb, kt=kt: e.matmul(
                                ps[:, oi * 2 + tb, :], wb[:, buf, oi * 128:(oi + 1) * 128],
                                xb[:, kt, tb * 512:(tb + 1) * 512], start=(kt == 0), stop=(kt == KT - 1)), deps=deps)
                    mml[k_lin] = h
                    qstate["k_lin"] = k_lin + 1
                for oi in range(width):
                    o = og * 4 + oi
                    for tb in range(2):
                        slot = oi * 2 + tb
                        if rope:
                            ci, si = (0, 1) if qtab else (2, 3)
                            p.emit("scalar", lambda e, slot=slot: e.copy(out=hf[:], in_=ps[:, slot, :]))
                            p.emit("tensor", lambda e, slot=slot: e.matmul(ps[:, slot, :], psw[:], hf[:], start=True, stop=True))
                            p.emit("vector", lambda e, ci=ci, tb=tb: e.tensor_mul(tf1[:], hf[:], tabs[:, ci, tb * 512:(tb + 1) * 512]))
                            p.emit("vector", lambda e, si=si, tb=tb, slot=slot: e.tensor_mul(tf2[:], ps[:, slot, :], tabs[:, si, tb * 512:(tb + 1) * 512]))
                            p.emit("vector", lambda e: e.tensor_add(obf[:], tf1[:], tf2[:]))
                        else:
                            p.emit("scalar", lambda e, slot=slot: e.copy(out=obf[:], in_=ps[:, slot, :]))
                        p.emit("gpsimd", lambda e, o=o, tb=tb, out_ap=out_ap: e.dma_start(
                            out=out_ap[o * 128:(o + 1) * 128, tb * 512:(tb + 1) * 512], in_=obf[:]), dma=True)
                qstate["evh"] = p.last
        qkv(wqi, 32, qbf, True, True)
        qkv(wki, 8, kbf, True, False)
        qkv(wvi, 8, vbf, False, False)
        p.build()
        _st.close()
    return nc


def build_e():
    nc = bass.Bass("TRN2", target_bir_lowering=False)
    T = 1024
    qi = nc.dram_tensor("qi", [D, T], BF16, kind="ExternalInput").ap()
    ki = nc.dram_tensor("ki", [1024, L], BF16, kind="ExternalInput").ap()
    vi = nc.dram_tensor("vi", [1024, L], BF16, kind="ExternalInput").ap()
    r2i = nc.dram_tensor("r2i", [D, T], F32, kind="ExternalInput").ap()
    woi = nc.dram_tensor("woi", [D, D], F32, kind="ExternalInput").ap()
    maski = nc.dram_tensor("maski", [8, 128, 512], F32, kind="ExternalInput").ap()
    identbi = nc.dram_tensor("identbi", [128, 128], BF16, kind="ExternalInput").ap()
    r3 = nc.dram_tensor("r3", [D, T], F32, kind="ExternalOutput").ap()
    _st = ExitStack()
    qh = _st.enter_context(nc.sbuf_tensor("qh", [128, 32, 512], BF16))
    kt_sb = _st.enter_context(nc.sbuf_tensor("kt_sb", [128, 8, L], BF16))
    vt_sb = _st.enter_context(nc.sbuf_tensor("vt_sb", [128, 8, L], BF16))
    vrow = _st.enter_context(nc.sbuf_tensor("vrow", [128, 16, 128], BF16))
    attn = _st.enter_context(nc.sbuf_tensor("attn", [128, 32, 512], BF16))
    srow = _st.enter_context(nc.sbuf_tensor("srow", [128, 2048], F32))
    pef = _st.enter_context(nc.sbuf_tensor("pef", [128, 2048], F32))
    pbb = _st.enter_context(nc.sbuf_tensor("pbb", [128, 2048], BF16))
    mask_sb = _st.enter_context(nc.sbuf_tensor("mask_sb", [128, 8, 512], F32))
    mcol = _st.enter_context(nc.sbuf_tensor("mcol", [128, 1], F32))
    lcol = _st.enter_context(nc.sbuf_tensor("lcol", [128, 1], F32))
    ptsb = _st.enter_context(nc.sbuf_tensor("ptsb", [128, 128], BF16))
    identb = _st.enter_context(nc.sbuf_tensor("identb", [128, 128], BF16))
    wt = _st.enter_context(nc.sbuf_tensor("wt", [128, 2, 512], F32))
    wb = _st.enter_context(nc.sbuf_tensor("wb", [128, 2, 512], BF16))
    r2t = _st.enter_context(nc.sbuf_tensor("r2t", [128, 512], F32))
    outf = _st.enter_context(nc.sbuf_tensor("outf", [128, 512], F32))
    psc = _st.enter_context(nc.psum_tensor("psc", [128, 512], F32))
    pav = _st.enter_context(nc.psum_tensor("pav", [128, 128], F32))
    ptb = _st.enter_context(nc.psum_tensor("ptb", [128, 128], BF16))
    pwo = _st.enter_context(nc.psum_tensor("pwo", [128, 4, 512], F32))
    if True:
        p = Prog(nc)
        p.emit("sync", lambda e: e.dma_start(out=identb[:], in_=identbi), dma=True)
        for s in range(8):
            p.emit("sync", lambda e, s=s: e.dma_start(out=mask_sb[:, s, :], in_=maski[s]), dma=True)
        for kvh in range(8):
            p.emit("sync", lambda e, kvh=kvh: e.dma_start(out=kt_sb[:, kvh, :], in_=ki[kvh * 128:(kvh + 1) * 128, :]), dma=True)
            p.emit("sync", lambda e, kvh=kvh: e.dma_start(out=vt_sb[:, kvh, :], in_=vi[kvh * 128:(kvh + 1) * 128, :]), dma=True)
        for half in range(2):
            for h in range(32):
                p.emit("sync", lambda e, h=h, half=half: e.dma_start(
                    out=qh[:, h, :], in_=qi[h * 128:(h + 1) * 128, half * 512:(half + 1) * 512]), dma=True)
            for kvh in range(8):
                for kt in range(16):
                    p.emit("tensor", lambda e, kvh=kvh, kt=kt: e.transpose(
                        ptb[:], vt_sb[:, kvh, kt * 128:(kt + 1) * 128], identb[:]))
                    p.emit("vector", lambda e, kt=kt: e.tensor_copy(out=vrow[:, kt, :], in_=ptb[:]))
                for hh in range(4):
                    h = kvh * 4 + hh
                    for s in range(4):
                        ss = half * 4 + s
                        nkb = (ss + 2) // 2
                        nkt = 2 * ss + 2
                        for kb in range(nkb):
                            p.emit("tensor", lambda e, h=h, s=s, kvh=kvh, kb=kb: e.matmul(
                                psc[:], qh[:, h, s * 128:(s + 1) * 128], kt_sb[:, kvh, kb * 512:(kb + 1) * 512],
                                start=True, stop=True))
                            if kb < nkb - 1:
                                p.emit("scalar", lambda e, kb=kb: e.copy(out=srow[:, kb * 512:(kb + 1) * 512], in_=psc[:]))
                            else:
                                p.emit("vector", lambda e, kb=kb, ss=ss: e.tensor_add(
                                    srow[:, kb * 512:(kb + 1) * 512], psc[:], mask_sb[:, ss, :]))
                        W = nkb * 512
                        p.emit("vector", lambda e, W=W: e.reduce_max(mcol[:], srow[:, 0:W], axis=AX.X))
                        p.emit("scalar", lambda e: e.mul(mcol[:], mcol[:], -1.0))
                        p.emit("scalar", lambda e, W=W: e.activation(pef[:, 0:W], srow[:, 0:W], AF.Exp, bias=mcol[:]))
                        p.emit("vector", lambda e, W=W: e.reduce_sum(lcol[:], pef[:, 0:W], axis=AX.X))
                        p.emit("vector", lambda e: e.reciprocal(lcol[:], lcol[:]))
                        p.emit("vector", lambda e, W=W: e.tensor_scalar(pbb[:, 0:W], pef[:, 0:W], lcol[:], None, op0=ALU.mult))
                        for kt in range(nkt):
                            p.emit("tensor", lambda e, kt=kt: e.transpose(ptb[:], pbb[:, kt * 128:(kt + 1) * 128], identb[:]))
                            p.emit("vector", lambda e: e.tensor_copy(out=ptsb[:], in_=ptb[:]))
                            p.emit("tensor", lambda e, kt=kt, nkt=nkt: e.matmul(
                                pav[:], vrow[:, kt, :], ptsb[:], start=(kt == 0), stop=(kt == nkt - 1)))
                        p.emit("scalar", lambda e, h=h, s=s: e.copy(out=attn[:, h, s * 128:(s + 1) * 128], in_=pav[:]))
            casts, mml = {}, {}
            k_lin = 0
            evh = p.last
            for og in range(8):
                for kt in range(32):
                    buf = k_lin % 2
                    dh = p.emit("sync" if buf == 0 else "gpsimd", lambda e, kt=kt, og=og, buf=buf: e.dma_start(
                        out=wt[:, buf, :], in_=woi[kt * 128:(kt + 1) * 128, og * 512:(og + 1) * 512]),
                        dma=True, deps=[casts.get(k_lin - 2)], chan="wA" if buf == 0 else "wB")
                    ch = p.emit("vector", lambda e, buf=buf: e.tensor_copy(out=wb[:, buf, :], in_=wt[:, buf, :]),
                                deps=[dh, mml.get(k_lin - 2)])
                    casts[k_lin] = ch
                    for oi in range(4):
                        deps = [ch, evh] if kt == 0 else [ch]
                        h = p.emit("tensor", lambda e, buf=buf, oi=oi, kt=kt: e.matmul(
                            pwo[:, oi, :], wb[:, buf, oi * 128:(oi + 1) * 128], attn[:, kt, :],
                            start=(kt == 0), stop=(kt == 31)), deps=deps)
                    mml[k_lin] = h
                    k_lin += 1
                first = True
                for oi in range(4):
                    o = og * 4 + oi
                    p.emit("sync", lambda e, o=o, half=half: e.dma_start(
                        out=r2t[:], in_=r2i[o * 128:(o + 1) * 128, half * 512:(half + 1) * 512]),
                        dma=True, deps=([mml[k_lin - 1]] if first else "prev"))
                    first = False
                    p.emit("vector", lambda e, oi=oi: e.tensor_add(outf[:], pwo[:, oi, :], r2t[:]))
                    p.emit("gpsimd", lambda e, o=o, half=half: e.dma_start(
                        out=r3[o * 128:(o + 1) * 128, half * 512:(half + 1) * 512], in_=outf[:]), dma=True)
                evh = p.last
        p.build()
        _st.close()
    return nc


def build_f2():
    nc = bass.Bass("TRN2", target_bir_lowering=False)
    T = 1024
    r3i = nc.dram_tensor("r3i", [D, T], F32, kind="ExternalInput").ap()
    gdqi = nc.dram_tensor("gdqi", [D, INTER], BF16, kind="ExternalInput").ap()
    udqi = nc.dram_tensor("udqi", [D, INTER], BF16, kind="ExternalInput").ap()
    ddqi = nc.dram_tensor("ddqi", [INTER, D], BF16, kind="ExternalInput").ap()
    outo = nc.dram_tensor("outo", [D, T], F32, kind="ExternalOutput").ap()
    KT = D // 128
    IT = INTER // 128
    _st = ExitStack()
    xb = _st.enter_context(nc.sbuf_tensor("xb", [128, KT, T], BF16))
    xtmp = _st.enter_context(nc.sbuf_tensor("xtmp", [128, T], F32))
    sq = _st.enter_context(nc.sbuf_tensor("sq", [128, T], F32))
    rowbuf = _st.enter_context(nc.sbuf_tensor("rowbuf", [1, 3 * T], F32))
    bcast = _st.enter_context(nc.sbuf_tensor("bcast", [128, 2 * T], F32))
    wb = _st.enter_context(nc.sbuf_tensor("wb", [128, 2, 512], BF16))
    hsb = _st.enter_context(nc.sbuf_tensor("hsb", [128, IT, 512], BF16))
    gact = _st.enter_context(nc.sbuf_tensor("gact", [128, 4, 512], F32))
    r3t = _st.enter_context(nc.sbuf_tensor("r3t", [128, 512], F32))
    outf = _st.enter_context(nc.sbuf_tensor("outf", [128, 512], F32))
    ones_col = _st.enter_context(nc.sbuf_tensor("ones_col", [128, 1], F32))
    ones_row = _st.enter_context(nc.sbuf_tensor("ones_row", [1, 128], F32))
    eps_col = _st.enter_context(nc.sbuf_tensor("eps_col", [1, 1], F32))
    ps = _st.enter_context(nc.psum_tensor("ps", [128, 8, 512], F32))
    if True:
        p = Prog(nc)
        p.emit("vector", lambda e: e.memset(ones_col[:], 1.0))
        p.emit("vector", lambda e: e.memset(ones_row[:], 1.0))
        p.emit("vector", lambda e: e.memset(eps_col[:], EPS))
        emit_layernorm(p, r3i, KT, T, xtmp, sq, rowbuf, bcast, xb, ps, ones_col, ones_row, eps_col)
        mml = {}
        k_lin = 0
        evh = p.last
        for tb in range(2):
            for og in range(cdiv(IT, 4)):
                width = min(4, IT - og * 4)
                for mat, pbase in ((gdqi, 0), (udqi, 4)):
                    for kt in range(KT):
                        buf = k_lin % 2
                        dh = p.emit("sync" if buf == 0 else "gpsimd", lambda e, kt=kt, og=og, buf=buf, width=width, mat=mat: e.dma_start(
                            out=wb[:, buf, 0:width * 128], in_=mat[kt * 128:(kt + 1) * 128, og * 512:og * 512 + width * 128]),
                            dma=True, deps=[mml.get(k_lin - 2)], chan="wA" if buf == 0 else "wB")
                        for oi in range(width):
                            deps = [dh, evh] if kt == 0 else [dh]
                            h = p.emit("tensor", lambda e, buf=buf, oi=oi, kt=kt, tb=tb, pbase=pbase: e.matmul(
                                ps[:, pbase + oi, :], wb[:, buf, oi * 128:(oi + 1) * 128], xb[:, kt, tb * 512:(tb + 1) * 512],
                                start=(kt == 0), stop=(kt == KT - 1)), deps=deps)
                        mml[k_lin] = h
                        k_lin += 1
                    if pbase == 0:
                        first = True
                        for oi in range(width):
                            silu_h = p.emit("scalar", lambda e, oi=oi: e.activation(gact[:, oi, :], ps[:, oi, :], AF.Silu),
                                            deps=([mml[k_lin - 1], evh] if first else "prev"))
                            first = False
                    else:
                        first = True
                        for oi in range(width):
                            p.emit("vector", lambda e, oi=oi, og=og: e.tensor_mul(
                                hsb[:, og * 4 + oi, :], gact[:, oi, :], ps[:, 4 + oi, :]),
                                deps=([mml[k_lin - 1], silu_h] if first else "prev"))
                            first = False
                        evh = p.last
            for og in range(8):
                for kt in range(IT):
                    buf = k_lin % 2
                    dh = p.emit("sync" if buf == 0 else "gpsimd", lambda e, kt=kt, og=og, buf=buf: e.dma_start(
                        out=wb[:, buf, :], in_=ddqi[kt * 128:(kt + 1) * 128, og * 512:(og + 1) * 512]),
                        dma=True, deps=[mml.get(k_lin - 2)], chan="wA" if buf == 0 else "wB")
                    for oi in range(4):
                        deps = [dh, evh] if kt == 0 else [dh]
                        h = p.emit("tensor", lambda e, buf=buf, oi=oi, kt=kt: e.matmul(
                            ps[:, oi, :], wb[:, buf, oi * 128:(oi + 1) * 128], hsb[:, kt, :],
                            start=(kt == 0), stop=(kt == IT - 1)), deps=deps)
                    mml[k_lin] = h
                    k_lin += 1
                first = True
                for oi in range(4):
                    o = og * 4 + oi
                    p.emit("sync", lambda e, o=o, tb=tb: e.dma_start(
                        out=r3t[:], in_=r3i[o * 128:(o + 1) * 128, tb * 512:(tb + 1) * 512]),
                        dma=True, deps=([mml[k_lin - 1]] if first else "prev"))
                    first = False
                    p.emit("vector", lambda e, oi=oi: e.tensor_add(outf[:], ps[:, oi, :], r3t[:]))
                    p.emit("gpsimd", lambda e, o=o, tb=tb: e.dma_start(
                        out=outo[o * 128:(o + 1) * 128, tb * 512:(tb + 1) * 512], in_=outf[:]), dma=True)
                evh = p.last
        p.build()
        _st.close()
    return nc


def _host_consts():
    identb = np.eye(128).astype(BF16NP)
    identf = np.eye(128, dtype=np.float32)
    maskT = np.zeros((128, 2, 256), np.float32)
    for jj in range(2):
        for pp in range(128):
            maskT[pp, jj, :jj * 128 + pp] = -30000.0
    e8 = np.repeat(np.eye(8, dtype=np.float32), 16, axis=1)
    psw = np.zeros((128, 128), np.float32)
    for i in range(64):
        psw[2 * i + 1, 2 * i] = -1.0
        psw[2 * i, 2 * i + 1] = 1.0
    freqs = 1.0 / 10000.0 ** (np.arange(0, 128, 2) / 128.0)
    ang = np.outer(np.arange(L), freqs)
    C2 = np.repeat(np.cos(ang).T, 2, axis=0).astype(np.float32)
    S2 = np.repeat(np.sin(ang).T, 2, axis=0).astype(np.float32)
    return identb, identf, maskT, e8, psw, C2, S2


def _emasks(p):
    masks = np.zeros((8, 128, 512), np.float32)
    for ss in range(8):
        nkb = (ss + 2) // 2
        j = 2 * ss + p
        qg = 128 * j + np.arange(128)
        kg = (nkb - 1) * 512 + np.arange(512)
        masks[ss] = np.where(kg[None, :] > qg[:, None], -30000.0, 0.0)
    return masks


def kernel(**inp):
    ac = np.ascontiguousarray
    identb, identf, maskT, e8, psw, C2, S2 = _host_consts()
    cores = list(range(NCORE))

    g_gate = np.asarray(inp["L1_ffn_gate_g_idx"])[::128]
    g_up = np.asarray(inp["L1_ffn_up_g_idx"])[::128]
    g_down = np.asarray(inp["L1_ffn_down_g_idx"])[::128]
    gate_sc = np.take(np.asarray(inp["L1_ffn_gate_scales"], np.float32), g_gate, axis=0)
    gate_qz = np.take(np.asarray(inp["L1_ffn_gate_qzeros"]), g_gate, axis=0)
    up_sc = np.take(np.asarray(inp["L1_ffn_up_scales"], np.float32), g_up, axis=0)
    up_qz = np.take(np.asarray(inp["L1_ffn_up_qzeros"]), g_up, axis=0)
    down_sc = np.take(np.asarray(inp["L1_ffn_down_scales"], np.float32), g_down, axis=0)
    down_qz = np.take(np.asarray(inp["L1_ffn_down_qzeros"]), g_down, axis=0)
    NG, ND2 = INTER // 8, D // 8
    nc = build_f1()
    maps = []
    for c in cores:
        maps.append(dict(
            gqw=ac(inp["L1_ffn_gate_qweight"][:, c * NG:(c + 1) * NG]),
            gsc=ac(gate_sc[:, c * NG:(c + 1) * NG]),
            gqz=ac(gate_qz[:, c * NG // 8:(c + 1) * NG // 8]),
            uqw=ac(inp["L1_ffn_up_qweight"][:, c * NG:(c + 1) * NG]),
            usc=ac(up_sc[:, c * NG:(c + 1) * NG]),
            uqz=ac(up_qz[:, c * NG // 8:(c + 1) * NG // 8]),
            dqw=ac(inp["L1_ffn_down_qweight"][:, c * ND2:(c + 1) * ND2]),
            dsc=ac(down_sc[:, c * ND2:(c + 1) * ND2]),
            dqz=ac(down_qz[:, c * ND2 // 8:(c + 1) * ND2 // 8])))
    res = run_bass_kernel_spmd(nc, maps, core_ids=cores)
    gdq = np.concatenate([r["gdq"] for r in res.results], axis=1)
    udq = np.concatenate([r["udq"] for r in res.results], axis=1)
    ddq = np.concatenate([r["ddq"] for r in res.results], axis=1)

    xT = ac(np.asarray(inp["L0_attn_hidden_states"], np.float32).transpose(0, 2, 1))
    w_inT = ac(np.asarray(inp["L0_attn_in_proj_w"], np.float32).T)
    nc = build_a()
    maps = [dict(xT=ac(xT[c // 2][:, (c % 2) * 1024:(c % 2 + 1) * 1024]), w_inT=w_inT) for c in cores]
    res = run_bass_kernel_spmd(nc, maps, core_ids=cores)
    zxbc_b = np.empty((B, PROJ - NH, L), BF16NP)
    dt_b = np.empty((B, NH, L), np.float32)
    for c in cores:
        b, th = c // 2, c % 2
        zxbc_b[b][:, th * 1024:(th + 1) * 1024] = res.results[c]["zxbc"]
        dt_b[b][:, th * 1024:(th + 1) * 1024] = res.results[c]["dtout"]

    conv_w3 = np.asarray(inp["L0_attn_conv_w"], np.float32)[:, 0, :]
    conv_b = np.asarray(inp["L0_attn_conv_b"], np.float32)
    dt_bias = np.asarray(inp["L0_attn_dt_bias"], np.float32)
    A_log = np.asarray(inp["L0_attn_A_log"], np.float32)
    Dv = np.asarray(inp["L0_attn_D"], np.float32)
    norm_w = np.asarray(inp["L0_attn_norm_w"], np.float32)
    units_all = [(b, g) for b in range(B) for g in range(G)]
    nc = build_bc()
    maps = []
    for c in cores:
        units = units_all[4 * c:4 * c + 4]
        def rows3(arr, g, w1, o2, o3):
            return np.concatenate([arr[1024 * g:1024 * (g + 1)], arr[o2 + 128 * g:o2 + 128 * (g + 1)],
                                   arr[o3 + 128 * g:o3 + 128 * (g + 1)]])
        maps.append(dict(
            gatei=np.stack([zxbc_b[b, 1024 * g:1024 * (g + 1)] for (b, g) in units]),
            xbci=np.stack([rows3(zxbc_b[b], g, 1024, 16384, 17408)[8192:] if False else
                           np.concatenate([zxbc_b[b, 8192 + 1024 * g:8192 + 1024 * (g + 1)],
                                           zxbc_b[b, 16384 + 128 * g:16384 + 128 * (g + 1)],
                                           zxbc_b[b, 17408 + 128 * g:17408 + 128 * (g + 1)]]) for (b, g) in units]),
            dti=np.stack([dt_b[b, 16 * g:16 * (g + 1)] for (b, g) in units]),
            cw=np.stack([np.concatenate([conv_w3[1024 * g:1024 * (g + 1)], conv_w3[8192 + 128 * g:8192 + 128 * (g + 1)],
                                         conv_w3[9216 + 128 * g:9216 + 128 * (g + 1)]]) for (b, g) in units]),
            cb=np.stack([np.concatenate([conv_b[1024 * g:1024 * (g + 1)], conv_b[8192 + 128 * g:8192 + 128 * (g + 1)],
                                         conv_b[9216 + 128 * g:9216 + 128 * (g + 1)]])[:, None] for (b, g) in units]),
            dtb=np.stack([dt_bias[16 * g:16 * (g + 1)][:, None] for (b, g) in units]),
            alog=np.stack([A_log[16 * g:16 * (g + 1)][:, None] for (b, g) in units]),
            drow=np.stack([Dv[16 * g:16 * (g + 1)][None, :] for (b, g) in units]),
            nwi=np.stack([ac(norm_w[1024 * g:1024 * (g + 1)].reshape(8, 128).T) for (b, g) in units]),
            maskTi=maskT, identbi=identb, identfi=identf))
    res = run_bass_kernel_spmd(nc, maps, core_ids=cores)
    yn_b = np.empty((B, DIN, L), BF16NP)
    for c in cores:
        for u, (b, g) in enumerate(units_all[4 * c:4 * c + 4]):
            yn_b[b, 1024 * g:1024 * (g + 1)] = res.results[c]["yn"][u]

    opT = ac(np.asarray(inp["L0_attn_out_proj_w"], np.float32).T)
    nc = build_d1()
    maps = [dict(ynf=ac(yn_b[c // 2][:, (c % 2) * 1024:(c % 2 + 1) * 1024]),
                 xres=ac(xT[c // 2][:, (c % 2) * 1024:(c % 2 + 1) * 1024]), opT=opT) for c in cores]
    res = run_bass_kernel_spmd(nc, maps, core_ids=cores)
    r1_b = np.empty((B, D, L), np.float32)
    for c in cores:
        r1_b[c // 2][:, (c % 2) * 1024:(c % 2 + 1) * 1024] = res.results[c]["r1"]

    fWT = ac(np.asarray(inp["L0_ffn_W"], np.float32).T)
    laT = ac(np.asarray(inp["L0_ffn_lora_A"], np.float32).reshape(128, D).T)
    lbT = ac(np.transpose(np.asarray(inp["L0_ffn_lora_B"], np.float32), (0, 2, 1)).reshape(128, D))
    fbias = ac(np.asarray(inp["L0_ffn_bias"], np.float32).reshape(32, 128).T)
    lw = ac(np.asarray(inp["L0_ffn_lora_weights"], np.float32)[:, None])
    wq = ac(np.asarray(inp["L1_attn_wq"], np.float32))
    wk = ac(np.asarray(inp["L1_attn_wk"], np.float32))
    wv = ac(np.asarray(inp["L1_attn_wv"], np.float32))
    sc = np.float32(1.0 / np.sqrt(128.0))
    nc = build_d2()
    maps = []
    for c in cores:
        b, th = c // 2, c % 2
        sl = slice(th * 1024, (th + 1) * 1024)
        maps.append(dict(r1i=ac(r1_b[b][:, sl]), fWT=fWT, laT=laT, lbT=lbT, fbias=fbias, lw=lw, e8=e8,
                         wqi=wq, wki=wk, wvi=wv, c2q=ac(C2[:, sl] * sc), s2q=ac(S2[:, sl] * sc),
                         c2k=ac(C2[:, sl]), s2k=ac(S2[:, sl]), pswi=psw))
    res = run_bass_kernel_spmd(nc, maps, core_ids=cores)
    r2_b = np.empty((B, D, L), np.float32)
    q_b = np.empty((B, D, L), BF16NP)
    k_b = np.empty((B, 1024, L), BF16NP)
    v_b = np.empty((B, 1024, L), BF16NP)
    for c in cores:
        b, th = c // 2, c % 2
        sl = slice(th * 1024, (th + 1) * 1024)
        r2_b[b][:, sl] = res.results[c]["r2o"]
        q_b[b][:, sl] = res.results[c]["qbf"]
        k_b[b][:, sl] = res.results[c]["kbf"]
        v_b[b][:, sl] = res.results[c]["vbf"]

    wo = ac(np.asarray(inp["L1_attn_wo"], np.float32))
    nc = build_e()
    maps = []
    for c in cores:
        b, p = c // 2, c % 2
        maps.append(dict(qi=ac(q_b[b].reshape(D, 16, 128)[:, p::2].reshape(D, 1024)),
                         ki=k_b[b], vi=v_b[b],
                         r2i=ac(r2_b[b].reshape(D, 16, 128)[:, p::2].reshape(D, 1024)),
                         woi=wo, maski=_emasks(p), identbi=identb))
    res = run_bass_kernel_spmd(nc, maps, core_ids=cores)
    r3_list = [res.results[c]["r3"] for c in cores]

    nc = build_f2()
    maps = [dict(r3i=r3_list[c], gdqi=gdq, udqi=udq, ddqi=ddq) for c in cores]
    res = run_bass_kernel_spmd(nc, maps, core_ids=cores)
    out_b = np.empty((B, D, L), np.float32)
    for c in cores:
        b, p = c // 2, c % 2
        out_b[b].reshape(D, 16, 128)[:, p::2] = res.results[c]["outo"].reshape(D, 8, 128)
    return ac(out_b.transpose(0, 2, 1))
